# version 05 (final)
# baseline (speedup 1.0000x reference)
import numpy as np
import concourse.bass as bass
import concourse.mybir as mybir
from concourse.bass_utils import run_bass_kernel_spmd
from concourse.bass_core_rust import IndirectOffsetOnAxis

F32 = mybir.dt.float32
BF16 = mybir.dt.bfloat16
I32 = mybir.dt.int32
AF = mybir.ActivationFunctionType
ALU = mybir.AluOpType
AX = mybir.AxisListType

D = 2048
ZCH = 32
DCH = 16
NE = 8
ICH = 8
NCORES = 8
TCORE = 4096
TT = 512
NSUB = 4
NT = TCORE // TT
RING_SLOTS = 3
EPS = 1e-5

ENGINES = ("sync", "gpsimd", "vector", "scalar", "tensor")


class Sched:

    def __init__(self):
        self.ops = {0: {e: [] for e in ENGINES}, 1: {e: [] for e in ENGINES}}
        self.cnt = {}
        self.lab = {}

    def add(self, blk, eng, fn, wait=(), inc=None, inc_n=1, label=None, inc2=None,
            psw=(), psr=()):
        self.ops[blk][eng].append([fn, list(wait), inc, inc_n, label, inc2,
                                   frozenset(psw), frozenset(psr)])
        if inc is not None:
            self.cnt[inc] = self.cnt.get(inc, 0) + inc_n
            if label is not None:
                self.lab[label] = (inc, self.cnt[inc])
        else:
            assert label is None
        if inc2 is not None:
            self.cnt[inc2] = self.cnt.get(inc2, 0) + 1

    def resolve_waits(self):
        for blk in (0, 1):
            for e in ENGINES:
                for op in self.ops[blk][e]:
                    rw = []
                    for w in op[1]:
                        if isinstance(w, str):
                            assert w in self.lab, f"unknown label {w}"
                            rw.append(self.lab[w])
                        elif isinstance(w, tuple) and len(w) == 2 and w[0] == "+1":
                            s, v = self.lab[w[1]]
                            assert v + 1 <= self.cnt[s], f"+1 wait beyond total: {w[1]}"
                            rw.append((s, v + 1))
                        else:
                            rw.append(w)
                    op[1] = rw

    def verify(self):
        import bisect
        sem = {}
        semvc = {}
        eng_i = {e: ENGINES.index(e) for e in ENGINES}
        NV = len(ENGINES)
        vc_cur = {e: [-1] * NV for e in ENGINES}
        idx_cur = {e: 0 for e in ENGINES}
        banks = {b: {"w": None, "r": []} for b in range(8)}
        races = []
        lab_by_val = {}
        for name, (s, v) in self.lab.items():
            lab_by_val[(s, v)] = name

        def hb(y, x_vc):
            return y[1] <= x_vc[eng_i[y[0]]]

        for blk in (0, 1):
            ptr = {e: 0 for e in ENGINES}
            while True:
                progressed = False
                for e in ENGINES:
                    while ptr[e] < len(self.ops[blk][e]):
                        fn, waits, incsem, inc_n, lab, inc2, psw, psr = self.ops[blk][e][ptr[e]]
                        if not all(sem.get(s, 0) >= v for s, v in waits):
                            break
                        vc = list(vc_cur[e])
                        for s, v in waits:
                            cs, vs = semvc.get(s, ([], []))
                            j = bisect.bisect_left(cs, v)
                            if j < len(cs):
                                wvc = vs[j]
                                for q in range(NV):
                                    vc[q] = max(vc[q], wvc[q])
                        my = (e, idx_cur[e])
                        vc[eng_i[e]] = idx_cur[e]
                        for b in psw | psr:
                            st = banks[b]
                            if st["w"] is not None and st["w"][0][0] != e and not hb(st["w"][0], vc):
                                races.append((b, st["w"][1], lab or f"{e}#{idx_cur[e]}", "w-then-acc"))
                            if b in psw:
                                for racc in st["r"]:
                                    if racc[0][0] != e and not hb(racc[0], vc):
                                        races.append((b, racc[1], lab or f"{e}#{idx_cur[e]}", "r-then-w"))
                                banks[b] = {"w": (my, lab or f"{e}#{idx_cur[e]}"), "r": []}
                            else:
                                st["r"].append((my, lab or f"{e}#{idx_cur[e]}"))
                        if incsem:
                            sem[incsem] = sem.get(incsem, 0) + inc_n
                            cs, vs = semvc.setdefault(incsem, ([], []))
                            cs.append(sem[incsem]); vs.append(list(vc))
                        if inc2:
                            sem[inc2] = sem.get(inc2, 0) + 1
                            cs, vs = semvc.setdefault(inc2, ([], []))
                            cs.append(sem[inc2]); vs.append(list(vc))
                        vc_cur[e] = vc
                        idx_cur[e] += 1
                        ptr[e] += 1
                        progressed = True
                if all(ptr[e] == len(self.ops[blk][e]) for e in ENGINES):
                    break
                if not progressed:
                    detail = {}
                    for e in ENGINES:
                        if ptr[e] < len(self.ops[blk][e]):
                            op = self.ops[blk][e][ptr[e]]
                            unmet = [(s, v, sem.get(s, 0),
                                      str(lab_by_val.get((s, v)) or lab_by_val.get((s, v - 1)) or "?"))
                                     for s, v in op[1] if sem.get(s, 0) < v]
                            detail[e] = (op[4], unmet)
                    raise RuntimeError(f"DEADLOCK in block {blk}: {detail}")
        if races:
            raise RuntimeError("PSUM RACES:\n" + "\n".join(
                f"  bank {b}: {a} vs {c} ({kind})" for b, a, c, kind in races[:20]))


def build_program(n_tiles=NT, debug=False, stop_after_f=False):
    nc = bass.Bass("TRN2", target_bir_lowering=False)
    S = Sched()

    x_in = nc.dram_tensor("L0_attn_x", [TCORE, D], F32, kind="ExternalInput").ap()
    gamma = nc.dram_tensor("L0_attn_gamma", [D], F32, kind="ExternalInput").ap()
    beta = nc.dram_tensor("L0_attn_beta", [D], F32, kind="ExternalInput").ap()
    w_in = nc.dram_tensor("L0_attn_W_in", [8192, D], F32, kind="ExternalInput").ap()
    w_out = nc.dram_tensor("L0_attn_W_out", [D, 4096], F32, kind="ExternalInput").ap()
    dt_in = nc.dram_tensor("L0_attn_dt", [4096], F32, kind="ExternalInput").ap()
    rg = nc.dram_tensor("L0_ffn_router_gate", [NE, D], F32, kind="ExternalInput").ap()
    gu = nc.dram_tensor("L0_ffn_expert_gate_up", [NE, D, D], F32, kind="ExternalInput").ap()
    dwn = nc.dram_tensor("L0_ffn_expert_down", [NE, 1024, D], F32, kind="ExternalInput").ap()
    out_h = nc.dram_tensor("out", [TCORE, D], F32, kind="ExternalOutput").ap()

    w_in_bf = nc.dram_tensor("w_in_bf", [8192, D], BF16, kind="Internal").ap()
    w_out_bf = nc.dram_tensor("w_out_bf", [D, 4096], BF16, kind="Internal").ap()
    winT = nc.dram_tensor("winT", [32, 128, 4096], BF16, kind="Internal").ap()
    woutT = nc.dram_tensor("woutT", [16, 128, 4096], BF16, kind="Internal").ap()
    gu_bf = nc.dram_tensor("gu_bf", [NE, 8, 128, 4096], BF16, kind="Internal").ap()
    dwn_bf = nc.dram_tensor("dwn_bf", [NE, 2, 2, 128, 4096], BF16, kind="Internal").ap()
    dtw_scr = nc.dram_tensor("dtw_scr", [4096], F32, kind="Internal").ap()
    xn_tok = nc.dram_tensor("xn_tok", [TCORE, D], BF16, kind="Internal").ap()
    s_tok = nc.dram_tensor("s_tok", [TCORE, D], BF16, kind="Internal").ap()
    yg_scr = nc.dram_tensor("yg_scr", [NE * 768, D], BF16, kind="Internal").ap()
    idx_scr = nc.dram_tensor("idx_scr", [NE * 768, 1], I32, kind="Internal").ap()

    PB = nc.alloc_sbuf_tensor("PB", [128, 32768], BF16)
    PA = nc.alloc_sbuf_tensor("PA", [128, 16384], BF16)
    xnT_hi = nc.alloc_sbuf_tensor("xnT_hi", [128, 8192], BF16)
    xnT_lo = nc.alloc_sbuf_tensor("xnT_lo", [128, 8192], BF16)
    PC = nc.alloc_sbuf_tensor("PC", [128, 8192], BF16)
    ring = nc.alloc_sbuf_tensor("ring", [128, RING_SLOTS * 4096], BF16)
    tmp = nc.alloc_sbuf_tensor("tmp", [128, 2048], F32)
    w_sb = nc.alloc_sbuf_tensor("w_sb", [128, 512], F32)
    w_sbB = nc.alloc_sbuf_tensor("w_sbB", [128, 512], F32)
    ident = nc.alloc_sbuf_tensor("ident", [128, 128], F32)
    identb = nc.alloc_sbuf_tensor("identb", [128, 128], BF16)
    selt = nc.alloc_sbuf_tensor("selt", [8, 1024], BF16)
    rT_hi = nc.alloc_sbuf_tensor("rT_hi", [128, 128], BF16)
    rT_lo = nc.alloc_sbuf_tensor("rT_lo", [128, 128], BF16)
    gcol = nc.alloc_sbuf_tensor("gcol", [128, 16], F32)
    bcol = nc.alloc_sbuf_tensor("bcol", [128, 16], F32)
    dtw_col = nc.alloc_sbuf_tensor("dtw_col", [128, 32], F32)
    logitsT = nc.alloc_sbuf_tensor("logitsT", [8, 512], F32)
    L_tok = nc.alloc_sbuf_tensor("L_tok", [128, 32], F32)
    oh1 = nc.alloc_sbuf_tensor("oh1", [128, 32], F32)
    oh2 = nc.alloc_sbuf_tensor("oh2", [128, 32], F32)
    lmsk = nc.alloc_sbuf_tensor("lmsk", [128, 8], F32)
    w_tok = nc.alloc_sbuf_tensor("w_tok", [128, 32], F32)
    wT = nc.alloc_sbuf_tensor("wT", [8, 512], BF16)
    wTf = nc.alloc_sbuf_tensor("wTf", [8, 512], F32)
    ones8 = nc.alloc_sbuf_tensor("ones8", [8, 512], F32)
    maskT = nc.alloc_sbuf_tensor("maskT", [8, 512], F32)
    cntf = nc.alloc_sbuf_tensor("cntf", [8, 512], F32)
    cnt_init = nc.alloc_sbuf_tensor("cnt_init", [8, 2], F32)
    cntT_sb = nc.alloc_sbuf_tensor("cntT_sb", [128, 32], F32)
    baseb2 = nc.alloc_sbuf_tensor("baseb2", [128, 8], F32)
    tokid_sb = nc.alloc_sbuf_tensor("tokid_sb", [128, 32], I32)
    oobc = nc.alloc_sbuf_tensor("oobc", [128, 48], I32)
    pos_keep = nc.alloc_sbuf_tensor("pos_keep", [128, 32], I32)
    pos_g = nc.alloc_sbuf_tensor("pos_g", [128, 32], I32)
    posf_keep = nc.alloc_sbuf_tensor("posf_keep", [128, 32], F32)
    wv_keep = nc.alloc_sbuf_tensor("wv_keep", [128, 32], F32)
    idx_sb = nc.alloc_sbuf_tensor("idx_sb", [128, 48], I32)
    rt1 = nc.alloc_sbuf_tensor("rt1", [128, 8], F32)
    rti = nc.alloc_sbuf_tensor("rti", [128, 8], I32)
    rtb = nc.alloc_sbuf_tensor("rtb", [128, 8], I32)
    rtr = nc.alloc_sbuf_tensor("rtr", [128, 8], I32)
    rt2 = nc.alloc_sbuf_tensor("rt2", [128, 8], F32)
    rt3 = nc.alloc_sbuf_tensor("rt3", [128, 8], F32)
    st1 = nc.alloc_sbuf_tensor("st1", [128, 16], F32)
    st2 = nc.alloc_sbuf_tensor("st2", [128, 16], F32)
    sm = nc.alloc_sbuf_tensor("sm", [1, 8], F32)

    PA32 = PA.bitcast(F32)
    PB32 = PB.bitcast(F32)
    PAi = PA.bitcast(I32)

    xnA = PA32[:, 0:4096]
    xn2nA = PA32[:, 4096:8192]
    act_s = PA
    hA = PB[:, 0:16384]
    hB = PB[:, 16384:32768]
    out_sb = PB32[:, 0:8192]
    x_sb = PB32[:, 8192:16384]
    g_sb = PC
    xn2T = PC
    T0, T1, T2, T3 = (tmp[:, i * 512:(i + 1) * 512] for i in range(4))
    T2g = [T2, T3]
    w_sb2 = [w_sb, w_sbB]
    CAP = 768
    BPE = CAP // 128
    NCC = ((0, 512), (512, 256))
    xgT_a = xnT_hi
    xgT_b = xnT_lo[:, 0:4096]
    h_rt = PC[:, 0:6144]
    ygT = PA[:, 0:12288]
    xg_tok = [PB[:, 0:2048], PB[:, 2048:4096]]
    yg_stage = PB[:, 16384:28672]
    s_stage = PB[:, 0:8192]
    s_ld = PC
    m1b = PA[:, 0:2048]
    m2b = PA[:, 2048:4096]
    moef = PA32[:, 2048:4096]
    sgf = PA32[:, 4096:6144]

    dt_row = PA32[0:1, 0:4096]
    dtn_row = PA32[0:1, 4096:8192]
    iot = PAi[0:128, 0:128]
    seli = PAi[0:8, 128:1152]
    stW0, stW1 = PB[:, 0:2048], PB[:, 2048:4096]
    stO0, stO1 = PB[:, 4096:5120], PB[:, 5120:6144]
    gamma_st = PB32[0:16, 4096:4224]
    beta_st = PB32[0:16, 4224:4352]
    rg_st = PB32[0:8, 4352:6400]
    dtw32 = PB32[0:32, 6400:6528]

    psum = [nc.alloc_psum_tensor(f"pb{i}", [128, 512], F32) for i in range(8)]

    sem_names = ["ldw", "ldx", "stx", "prep", "pe", "sfree", "ve", "sc", "gp"]

    A = S.add

    def dma(eng_fn):
        return eng_fn

    A(1, "gpsimd", lambda e: e.iota(iot, [[1, 128]], channel_multiplier=-1,
                                    allow_small_or_imprecise_dtypes=True),
      inc="gp", label="p_iot")
    A(1, "gpsimd", lambda e: e.iota(seli, [[1, 8], [0, 128]], channel_multiplier=-1,
                                    allow_small_or_imprecise_dtypes=True),
      inc="gp", label="p_seli")
    A(1, "gpsimd", lambda e: e.dma_start(dt_row, dt_in.rearrange("(a b) -> a b", a=1)),
      wait=["p_ident", "p_sel"], inc="prep", inc_n=16, label="p_dt")
    A(1, "gpsimd", lambda e: e.dma_start(gamma_st, gamma.rearrange("(a b) -> a b", a=16)),
      inc="prep", inc_n=16, label="p_gam")
    A(1, "gpsimd", lambda e: e.dma_start(beta_st, beta.rearrange("(a b) -> a b", a=16)),
      inc="prep", inc_n=16, label="p_bet")
    A(1, "gpsimd", lambda e: e.dma_start(rg_st, rg), inc="prep", inc_n=16, label="p_rg")
    for p in range(4):
        A(1, "gpsimd",
          (lambda p: lambda e: e.dma_start(w_in_bf[p * 2048:(p + 1) * 2048, :],
                                           w_in[p * 2048:(p + 1) * 2048, :]))(p),
          inc="prep", inc_n=16, label=f"p_winbf{p}")
    for p in range(2):
        A(1, "gpsimd",
          (lambda p: lambda e: e.dma_start(w_out_bf[p * 1024:(p + 1) * 1024, :],
                                           w_out[p * 1024:(p + 1) * 1024, :]))(p),
          inc="prep", inc_n=16, label=f"p_woutbf{p}")
    gu_src = gu.rearrange("e (k p) (g cc) -> e g p k cc", p=128, cc=256)
    gu_dst = gu_bf.rearrange("e g p (k cc) -> e g p k cc", cc=256)
    for ex in range(NE):
        for g_ in range(8):
            A(1, "gpsimd", (lambda ex, g_: lambda e: e.dma_start(
                gu_dst[ex, g_], gu_src[ex, g_]))(ex, g_),
              inc="prep", inc_n=16,
              label=f"p_gubf{ex}" if g_ == 7 else None)
    dwn_src = dwn.rearrange("e (k p) (h q cc) -> e h q p k cc", p=128, h=2, cc=512)
    dwn_dst = dwn_bf.rearrange("e h q p (k cc) -> e h q p k cc", cc=512)
    for ex in range(NE):
        for dh_ in range(2):
            for q_ in range(2):
                A(1, "gpsimd", (lambda ex, dh_, q_: lambda e: e.dma_start(
                    dwn_dst[ex, dh_, q_], dwn_src[ex, dh_, q_]))(ex, dh_, q_),
                  inc="prep", inc_n=16,
                  label=f"p_dwnbf{ex}" if (dh_ == 1 and q_ == 1) else None)

    def vsw():
        return ("ve", S.cnt.get("ve", 0))

    A(1, "vector", lambda e: e.tensor_single_scalar(ident[:, :], iot, 0, ALU.is_equal),
      wait=["p_iot"], inc="ve", label="p_ident")
    A(1, "vector", lambda e: e.tensor_copy(identb[:, :], ident[:, :]),
      wait=[vsw()], inc="ve", label="p_identb")
    A(1, "vector", lambda e: e.tensor_single_scalar(selt[:, :], seli, 0, ALU.is_equal),
      wait=["p_seli"], inc="ve", label="p_sel")
    A(1, "vector", lambda e: e.tensor_scalar_mul(dtn_row, dt_row, -1.0),
      wait=["p_dt"], inc="ve", label="p_dtn")
    A(1, "vector", lambda e: e.reduce_max(sm[0:1, 0:1], dtn_row, axis=AX.X),
      wait=[vsw()], inc="ve")
    A(1, "vector", lambda e: e.tensor_scalar_mul(sm[0:1, 1:2], sm[0:1, 0:1], -1.0),
      wait=[vsw()], inc="ve", label="p_nm")
    A(1, "scalar", lambda e: e.activation(dt_row, dtn_row, AF.Exp, bias=sm[0:1, 1:2]),
      wait=["p_nm"], inc="sc", label="p_exp")
    A(1, "vector", lambda e: e.reduce_sum(sm[0:1, 2:3], dt_row, axis=AX.X),
      wait=["p_exp"], inc="ve")
    A(1, "vector", lambda e: e.reciprocal(sm[0:1, 3:4], sm[0:1, 2:3]),
      wait=[vsw()], inc="ve")
    A(1, "vector", lambda e: e.tensor_scalar_mul(dtn_row, dt_row, sm[0:1, 3:4]),
      wait=[vsw()], inc="ve", label="p_dtwrow")
    A(1, "gpsimd", lambda e: e.dma_start(dtw_scr.rearrange("(a b) -> a b", a=1), dtn_row),
      wait=["p_dtwrow"], inc="prep", inc_n=16, label="p_dtwst")
    A(1, "gpsimd", lambda e: e.dma_start(dtw32, dtw_scr.rearrange("(a b) -> a b", a=32)),
      wait=["p_dtwst"], inc="prep", inc_n=16, label="p_dtw32")

    A(1, "vector", lambda e: e.memset(ones8[:, :], 1.0), inc="ve")
    A(1, "vector", lambda e: e.memset(oobc[:, :], 1 << 24), inc="ve", label="p_oobc")
    dregs = {}

    def _setup_regs(e):
        dregs["tok"] = e.alloc_register("rb_tok")
        e.reg_mov(dregs["tok"], TCORE)
        dregs["yg"] = e.alloc_register("rb_yg")
        return e.reg_mov(dregs["yg"], NE * 768)
    A(1, "gpsimd", _setup_regs, inc="gp", label="p_dregs")
    A(1, "gpsimd", lambda e: e.iota(tokid_sb[:, :], [[128, 32]], channel_multiplier=1,
                                    allow_small_or_imprecise_dtypes=True),
      inc="gp", label="p_tokid")
    A(1, "gpsimd", lambda e: e.iota(PAi[0:128, 1280:1288], [[6, 8]], channel_multiplier=0,
                                    allow_small_or_imprecise_dtypes=True),
      inc="gp", label="p_bb2i")
    A(1, "vector", lambda e: e.tensor_single_scalar(baseb2[:, :], PAi[0:128, 1280:1288], 0, ALU.add),
      wait=["p_bb2i", "p_sel"], inc="ve", label="p_bb2")

    A(1, "tensor", lambda e: e.transpose(psum[0][:, 0:32], dtw32, ident[0:32, 0:32]),
      wait=["p_dtw32", "p_ident"], inc=None, psw={0})
    A(1, "tensor", lambda e: e.drain(), wait=[], inc="pe", label="p_dtwT")
    A(1, "vector", lambda e: e.tensor_copy(dtw_col[:, :], psum[0][:, 0:32]),
      wait=["p_dtwT"], inc="ve", label="p_dtwcol", psr={0})
    A(1, "tensor", lambda e: e.transpose(psum[1][:, 0:16], gamma_st, ident[0:16, 0:16]),
      wait=["p_gam", "p_ident"], inc=None, psw={1})
    A(1, "tensor", lambda e: e.drain(), wait=[], inc="pe", label="p_gamT")
    A(1, "tensor", lambda e: e.transpose(psum[1][:, 16:32], beta_st, ident[0:16, 0:16]),
      wait=["p_bet", "p_gcol"], inc=None, psw={1})
    A(1, "tensor", lambda e: e.drain(), wait=[], inc="pe", label="p_betT")
    A(1, "vector", lambda e: e.tensor_copy(gcol[:, :], psum[1][:, 0:16]),
      wait=["p_gamT"], inc="ve", label="p_gcol", psr={1})
    A(1, "vector", lambda e: e.tensor_copy(bcol[:, :], psum[1][:, 16:32]),
      wait=["p_betT"], inc="ve", label="p_bcol", psr={1})
    for c in range(16):
        A(1, "tensor",
          (lambda c: lambda e: e.transpose(psum[2][:, 8 * c:8 * c + 8],
                                           rg_st[0:8, 128 * c:128 * (c + 1)],
                                           ident[0:8, 0:8]))(c),
          wait=["p_rg", "p_ident"], inc=None, psw={2})
        A(1, "tensor", lambda e: e.drain(), wait=[], inc="pe", label=f"p_rgT{c}")
    A(1, "vector", lambda e: e.tensor_copy(rT_hi[:, :], psum[2][:, 0:128]),
      wait=["p_rgT15"], inc="ve", label="p_rthi", psr={2})
    A(1, "vector", lambda e: e.scalar_tensor_tensor(rT_lo[:, :], rT_hi[:, :], -1.0,
                                                    psum[2][:, 0:128], ALU.mult, ALU.add),
      wait=[vsw()], inc="ve", label="p_rtlo", psr={2})

    stW = [stW0, stW1]
    winT_f = winT.rearrange("g p (h k cc) -> g p h k cc", h=2, cc=128)
    for b in range(4):
        for k in range(16):
            j = b * 16 + k
            wt = [f"p_winbf{b}"]
            if j >= 2:
                wt.append(f"p_winst{j-2}")
            A(1, "sync",
              (lambda b, k, j: lambda e: e.dma_start(
                  stW[j % 2][:, :],
                  w_in_bf[b * 2048:(b + 1) * 2048, k * 128:(k + 1) * 128],
                  transpose=True))(b, k, j),
              wait=wt, inc="ldw", inc_n=16, label=f"p_wintr{j}")
            h = b // 2
            g0 = (b % 2) * 16
            dstap = winT_f[g0:g0 + 16, :, h, k, :].rearrange("c p cc -> p c cc")
            srcap = stW[j % 2][:, :].rearrange("p (c cc) -> p c cc", cc=128)
            A(1, "sync",
              (lambda dstap, srcap: lambda e: e.dma_start(dstap, srcap))(dstap, srcap),
              wait=[f"p_wintr{j}"], inc="stx", inc_n=16, label=f"p_winst{j}")
    woutT_f = woutT.rearrange("dc p (k cc) -> dc p k cc", cc=128)
    for k in range(32):
        for b in range(2):
            j = k * 2 + b
            wt = [f"p_woutbf{b}"]
            if j >= 2:
                wt.append(f"p_woutst{j-2}")
            A(1, "sync",
              (lambda b, k, j: lambda e: e.dma_start(
                  stW[j % 2][:, 0:1024],
                  w_out_bf[b * 1024:(b + 1) * 1024, k * 128:(k + 1) * 128],
                  transpose=True))(b, k, j),
              wait=wt, inc="ldw", inc_n=16, label=f"p_wouttr{j}")
            dstap = woutT_f[b * 8:(b + 1) * 8, :, k, :].rearrange("c p cc -> p c cc")
            srcap = stW[j % 2][:, 0:1024].rearrange("p (c cc) -> p c cc", cc=128)
            A(1, "sync",
              (lambda dstap, srcap: lambda e: e.dma_start(dstap, srcap))(dstap, srcap),
              wait=[f"p_wouttr{j}"], inc="stx", inc_n=16, label=f"p_woutst{j}")


    x_r = x_in.rearrange("(t s p) d -> t p s d", p=128, s=NSUB)
    out_r = out_h.rearrange("(t s p) d -> t p s d", p=128, s=NSUB)

    slab_idx = [0]

    def slab_load(fn, label, extra_wait=()):
        i = slab_idx[0]
        wt = list(extra_wait)
        if i >= RING_SLOTS:
            wt.append(("sfree", i - (RING_SLOTS - 1)))
        slot = ring[:, (i % RING_SLOTS) * 4096:((i % RING_SLOTS) + 1) * 4096]
        A(1, "sync", fn(slot), wait=wt, inc="ldw", inc_n=16, label=label)
        slab_idx[0] += 1
        return slot, i

    def stc(stv, base, j):
        return stv[:, base + j:base + j + 1]

    def g1_transposes_factory(tg, t):
        def g1_transposes(dc):
            bnk = psum[dc % 2]
            for s in range(NSUB):
                wt = [f"{tg}_gp{dc}"]
                if s == 0 and dc >= 2:
                    wt.append(f"{tg}_ga{dc-2}")
                A(1, "tensor", (lambda bnk, s, dc: lambda e: e.transpose(
                    bnk[:, s * 128:(s + 1) * 128], T2g[dc % 2][:, s * 128:(s + 1) * 128],
                    ident[:, :]))(bnk, s, dc),
                  wait=wt, inc=None, psw={dc % 2})
            A(1, "tensor", lambda e: e.drain(), wait=[], inc="pe", label=f"{tg}_gq{dc}")
            gaw = [f"{tg}_gq{dc}"]
            if dc == 0:
                if t % 4 > 0:
                    gaw.append(f"t{t-1}_sspill")
                elif t > 0:
                    gaw.append(f"t{t-1}_store")
            for s4 in range(NSUB):
                sdst = s_stage[:, s4 * 2048 + dc * 128: s4 * 2048 + (dc + 1) * 128]
                psrc = bnk[:, s4 * 128:(s4 + 1) * 128]
                A(1, "vector", (lambda sdst, psrc: lambda e: e.tensor_copy(sdst, psrc))(sdst, psrc),
                  wait=(gaw if s4 == 0 else []), inc="ve",
                  label=f"{tg}_ga{dc}" if s4 == 3 else None, psr={dc % 2})
        return g1_transposes

    def emit_moe_supertile(st):
        mg = f"m{st}"
        t3 = 4 * st + 3
        A(1, "gpsimd", lambda e: e.dma_start(
            idx_sb[:, :], idx_scr.rearrange("(p c) o -> p (c o)", c=48)),
          wait=[("gp", S.cnt["gp"])], inc="ldx", inc_n=16, label=f"{mg}_idxld")
        gctr = 0
        for ex in range(NE):
            eg = f"{mg}e{ex}"
            for b in range(BPE):
                gw = [f"{mg}_idxld"]
                if ex == 0 and b < 2:
                    gw.append(f"t{t3}_sspill")
                if b >= 2:
                    gw.append(f"{eg}_trq{b-2}")
                elif ex > 0:
                    gw.append(f"m{st}e{ex-1}_trq{BPE+b-2}")
                A(1, "gpsimd", (lambda ex, b: lambda e: e.indirect_dma_start(
                    xg_tok[b % 2][:, :], None,
                    xn_tok,
                    IndirectOffsetOnAxis(idx_sb[:, ex * BPE + b:ex * BPE + b + 1], 0),
                    bounds_check=dregs["tok"], oob_is_err=False))(ex, b),
                  wait=gw, inc="ldx", inc_n=16, label=f"{eg}_xg{b}")
                for g4 in range(4):
                    for cq in range(4):
                        c = g4 * 4 + cq
                        wt = [f"{eg}_xg{b}"]
                        if cq == 0:
                            if g4 > 0:
                                wt.append(f"{eg}_xgq{b}_{g4-1}")
                            elif ex == 0 and b < 2:
                                wt.append(f"t{t3}_ga{14 + b}")
                            elif b >= 2:
                                wt.append(f"{eg}_xgT{b-2}")
                            elif ex > 0:
                                wt.append(f"m{st}e{ex-1}_xgT{BPE+b-2}")
                        A(1, "tensor", (lambda b, c: lambda e: e.transpose(
                            psum[b % 2].bitcast(BF16)[:, (c % 4) * 128:(c % 4) * 128 + 128],
                            xg_tok[b % 2][:, c * 128:(c + 1) * 128], identb[:, :]))(b, c),
                          wait=wt if cq == 0 else [], inc=None, psw={b % 2})
                    A(1, "tensor", lambda e: e.drain(), wait=[], inc="pe",
                      label=f"{eg}_trq{b}" if g4 == 3 else None)
                    c = g4 * 4 + 3
                    if b < 4:
                        dstv = xgT_a.rearrange("p (k c) -> p k c", c=512)[
                            :, c - 3:c + 1, b * 128:b * 128 + 128]
                    else:
                        dstv = xgT_b.rearrange("p (k c) -> p k c", c=256)[
                            :, c - 3:c + 1, (b - 4) * 128:(b - 4) * 128 + 128]
                    psv = psum[b % 2].bitcast(BF16)[:, 0:512].rearrange("p (k c) -> p k c", c=128)
                    A(1, "vector", (lambda dstv, psv: lambda e: e.tensor_copy(dstv, psv))(dstv, psv),
                      wait=[("pe", S.cnt["pe"])] +
                           (["dbg_xtap"] if (debug and st == 0 and ex == 0 and b == 0 and g4 == 0) else []),
                      inc="ve",
                      label=(f"{eg}_xgT{b}" if g4 == 3 else f"{eg}_xgq{b}_{g4}"), psr={b % 2})
            for gp_ in range(ICH):
                slot, si = slab_load(
                    (lambda ex, gp_: lambda slot: (lambda e: e.dma_start(
                        slot[:, :], gu_bf[ex, gp_])))(ex, gp_),
                    f"{eg}_guld{gp_}")
                for icc, (ccoff, ccn) in enumerate(NCC):
                    ab_, bb_ = [(4, 5), (6, 7), (2, 3)][gctr % 3]
                    pa, pb_ = psum[ab_], psum[bb_]
                    rhs = (xgT_a.rearrange("p (k c) -> p k c", c=512) if ccn == 512
                           else xgT_b.rearrange("p (k c) -> p k c", c=256))
                    for k in range(16):
                        wt = [f"{eg}_guld{gp_}"]
                        if k == 0:
                            if gctr >= 3:
                                wt.append(f"{mg}_h{gctr-3}")
                            else:
                                wt.append(f"t{t3}_gp{ {0: 15, 1: 13, 2: 14}[gctr] }")
                            if ccn == 512:
                                wt += [f"{eg}_xgT0", f"{eg}_xgT1", f"{eg}_xgT2", f"{eg}_xgT3"]
                            else:
                                wt += [f"{eg}_xgT4", f"{eg}_xgT5"]
                        A(1, "tensor", (lambda slot, pa, k, rhs, ccn: lambda e: e.matmul(
                            pa[:, 0:ccn],
                            slot[:, k * 256:k * 256 + 256:2], rhs[:, k, :],
                            start=(k == 0), stop=(k == 15)))(slot, pa, k, rhs, ccn),
                          wait=wt if k == 0 else [], inc=None, psw={ab_})
                    for k in range(16):
                        A(1, "tensor", (lambda slot, pb_, k, rhs, ccn: lambda e: e.matmul(
                            pb_[:, 0:ccn],
                            slot[:, k * 256 + 1:k * 256 + 256:2], rhs[:, k, :],
                            start=(k == 0), stop=(k == 15)))(slot, pb_, k, rhs, ccn),
                          wait=[], inc="sfree" if (k == 15 and icc == 1) else None,
                          psw={bb_})
                    A(1, "tensor", lambda e: e.drain(), wait=[], inc="pe",
                      label=f"{mg}_gumm{gctr}")
                    hw = [f"{mg}_gumm{gctr}"]
                    if gctr >= 1:
                        hw.append(f"{mg}_ht2_{gctr-1}")
                    if ex > 0 and gctr == ex * 16:
                        hw.append(f"{mg}_dnq{ex-1}_3")
                    A(1, "scalar", (lambda pa, ccn: lambda e: e.activation(
                        T1[:, 0:ccn], pa[:, 0:ccn], AF.Sigmoid))(pa, ccn),
                      wait=hw, inc="sc", label=f"{mg}_hsg{gctr}", psr={ab_})
                    A(1, "vector", (lambda pa, ccn: lambda e: e.scalar_tensor_tensor(
                        T2[:, 0:ccn], pa[:, 0:ccn], 1.0, T1[:, 0:ccn],
                        ALU.mult, ALU.mult))(pa, ccn),
                      wait=[f"{mg}_hsg{gctr}"], inc="ve", label=f"{mg}_ht2_{gctr}", psr={ab_})
                    hcol = gp_ * 768 + ccoff
                    A(1, "vector", (lambda pb_, ccn, hcol: lambda e: e.tensor_mul(
                        h_rt[:, hcol:hcol + ccn], T2[:, 0:ccn], pb_[:, 0:ccn]))(pb_, ccn, hcol),
                      wait=[vsw()], inc="ve", label=f"{mg}_h{gctr}", psr={bb_})
                    gctr += 1
            for dh in range(2):
                slots = []
                for q in range(2):
                    slot, si = slab_load(
                        (lambda ex, dh, q: lambda slot: (lambda e: e.dma_start(
                            slot[:, :], dwn_bf[ex, dh, q])))(ex, dh, q),
                        f"{eg}_dnld{dh}_{q}")
                    slots.append((slot, si))
                for icc, (ccoff, ccn) in enumerate(NCC):
                    for q in range(2):
                        slot, si = slots[q]
                        for dcl in range(4):
                            bank = psum[q * 4 + dcl]
                            for k in range(8):
                                wt = []
                                if k == 0:
                                    wt.append(f"{eg}_dnld{dh}_{q}")
                                    if icc == 0 and dh == 0:
                                        wt.append(f"{mg}_h{ex * 16 + 15}")
                                    elif icc == 0:
                                        wt.append(f"{eg}_ydnq1_cc1")
                                    else:
                                        wt.append(f"{eg}_ydnq{2*dh+q}_cc0")
                                A(1, "tensor", (lambda slot, bank, k, dcl, ccoff, ccn, hcol0:
                                                lambda e: e.matmul(
                                    bank[:, 0:ccn],
                                    slot[:, k * 512 + dcl * 128:k * 512 + (dcl + 1) * 128],
                                    h_rt[:, k * 768 + ccoff:k * 768 + ccoff + ccn],
                                    start=(k == 0), stop=(k == 7)))(slot, bank, k, dcl, ccoff, ccn, 0),
                                  wait=wt, inc="sfree" if (icc == 1 and dcl == 3 and k == 7) else None,
                                  psw={q * 4 + dcl})
                        A(1, "tensor", lambda e: e.drain(), wait=[], inc="pe",
                          label=f"{eg}_dnq{2*dh+q}_cc{icc}" if dcl == 3 else None)
                    for q in range(2):
                        for dcl in range(4):
                            dglob = dh * 8 + q * 4 + dcl
                            ycol = dglob * 768 + ccoff
                            A(1, "vector", (lambda q, dcl, ycol, ccn: lambda e: e.tensor_copy(
                                ygT[:, ycol:ycol + ccn], psum[q * 4 + dcl][:, 0:ccn]))(q, dcl, ycol, ccn),
                              wait=[f"{eg}_dnq{2*dh+q}_cc{icc}"], inc="ve",
                              label=f"{eg}_ydnq{2*dh+q}_cc{icc}" if dcl == 3 else None,
                              psr={q * 4 + dcl})
                S.lab[f"{eg}_dnq{ex}_3" if False else f"{mg}_dnq{ex}_3"] = ("ve", S.cnt["ve"])
            for b in range(BPE):
                for d in range(16):
                    wt = []
                    if d == 0:
                        wt.append(f"{eg}_ydnq3_cc1")
                        if b >= 1:
                            wt.append(f"{eg}_ygst{b-1}_3")
                        elif ex > 0:
                            wt.append(f"m{st}e{ex-1}_ygw")
                    A(1, "tensor", (lambda b, d: lambda e: e.transpose(
                        psum[d // 4].bitcast(BF16)[:, (d % 4) * 128:(d % 4) * 128 + 128],
                        ygT[:, d * 768 + b * 128:d * 768 + b * 128 + 128],
                        identb[:, :]))(b, d),
                      wait=wt, inc=None, psw={d // 4})
                A(1, "tensor", lambda e: e.drain(), wait=[], inc="pe", label=f"{eg}_ytr{b}")
                for bk in range(4):
                    A(1, "vector", (lambda b, bk: lambda e: e.tensor_copy(
                        yg_stage[:, b * 2048 + bk * 512:b * 2048 + (bk + 1) * 512],
                        psum[bk].bitcast(BF16)[:, 0:512]))(b, bk),
                      wait=[f"{eg}_ytr{b}"], inc="ve",
                      label=f"{eg}_ygst{b}_{bk}", psr={bk})
            A(1, "gpsimd", (lambda ex: lambda e: e.dma_start(
                yg_scr.rearrange("(e b p) d -> e p b d", b=BPE, p=128)[ex],
                yg_stage.rearrange("p (b d) -> p b d", b=BPE)))(ex),
              wait=[f"{eg}_ygst{BPE-1}_3"], inc="stx", inc_n=16, label=f"{eg}_ygw")
        S.lab[f"{mg}_mend_pe"] = ("pe", S.cnt["pe"])

        for ct in range(4):
            tt = 4 * st + ct
            cg = f"c{tt}"
            cw = [f"{mg}e{NE-1}_ygw"]
            if ct > 0:
                cw.append(f"c{4*st+ct-1}_xdone")
            A(1, "gpsimd", (lambda tt: lambda e: e.dma_start(
                x_sb.rearrange("p (s d) -> p s d", s=NSUB), x_r[tt]))(tt),
              wait=cw, inc="ldx", inc_n=16, label=f"{cg}_xld")
            A(1, "gpsimd", (lambda tt: lambda e: e.dma_start(
                s_ld.rearrange("p (s d) -> p s d", s=NSUB),
                s_tok.rearrange("(t s p) d -> t p s d", p=128, s=NSUB)[tt]))(tt),
              wait=[f"{mg}e{NE-1}_ygw"] + ([f"c{4*st+ct-1}_sdone2"] if ct > 0 else []),
              inc="ldx", inc_n=16, label=f"{cg}_sld")
            for s in range(NSUB):
                for rk, mb in ((0, m1b), (1, m2b)):
                    kcol = ct * 8 + s * 2 + rk
                    mw = [f"{mg}e{NE-1}_ygw"]
                    if not (ct == 0 and s == 0):
                        mw.append(f"{cg}_moe{s-1}" if s > 0 else f"c{4*st+ct-1}_moe3")
                    A(1, "gpsimd", (lambda mb, kcol: lambda e: e.indirect_dma_start(
                        mb[:, :], None, yg_scr,
                        IndirectOffsetOnAxis(pos_g[:, kcol:kcol + 1], 0),
                        bounds_check=dregs["yg"], oob_is_err=False))(mb, kcol),
                      wait=mw, inc="ldx", inc_n=16, label=f"{cg}_mg{s}_{rk}")
                k1, k2 = ct * 8 + s * 2, ct * 8 + s * 2 + 1
                A(1, "vector", (lambda s, k1: lambda e: e.tensor_scalar_mul(
                    moef, m1b[:, :], wv_keep[:, k1:k1 + 1]))(s, k1),
                  wait=[f"{cg}_mg{s}_0", f"{cg}_mg{s}_1", ("ve", S.cnt["ve"])], inc="ve")
                A(1, "vector", (lambda s, k2: lambda e: e.scalar_tensor_tensor(
                    moef, m2b[:, :], wv_keep[:, k2:k2 + 1], moef, ALU.mult, ALU.add))(s, k2),
                  wait=[vsw()], inc="ve", label=f"{cg}_moe{s}")
                A(1, "scalar", (lambda: lambda e: e.activation(sgf, moef, AF.Sigmoid))(),
                  wait=[f"{cg}_moe{s}"], inc="sc", label=f"{cg}_sg{s}")
                A(1, "vector", lambda e: e.tensor_mul(sgf, sgf, moef),
                  wait=[f"{cg}_sg{s}"], inc="ve")
                A(1, "vector", (lambda s: lambda e: e.tensor_mul(
                    moef, sgf, s_ld[:, s * 2048:(s + 1) * 2048]))(s),
                  wait=[vsw(), f"{cg}_sld"], inc="ve")
                A(1, "vector", (lambda s: lambda e: e.tensor_add(
                    out_sb[:, s * 2048:(s + 1) * 2048], moef,
                    x_sb[:, s * 2048:(s + 1) * 2048]))(s),
                  wait=[vsw(), f"{cg}_xld"], inc="ve", label=f"{cg}_out{s}")
            S.lab[f"{cg}_xdone"] = ("ve", S.cnt["ve"])
            S.lab[f"{cg}_sdone2"] = ("ve", S.cnt["ve"])
            A(1, "gpsimd", (lambda tt: lambda e: e.dma_start(
                out_r[tt], out_sb.rearrange("p (s d) -> p s d", s=NSUB)))(tt),
              wait=[f"{cg}_out3"], inc="stx", inc_n=16, label=f"t{tt}_store")
        S.lab[f"{mg}_comb_done"] = ("ve", S.cnt["ve"])

    for t in range(n_tiles):
        tg = f"t{t}"
        st_t0 = t % 4
        xw = []
        if t > 0:
            if st_t0 == 0:
                xw = [f"m{t//4-1}_comb_done", f"t{t-1}_store"]
            else:
                xw = [f"t{t-1}_b1ln"]
        A(1, "gpsimd", (lambda t: lambda e: e.dma_start(
            x_sb.rearrange("p (s d) -> p s d", s=NSUB), x_r[t]))(t),
          wait=xw, inc="ldx", inc_n=16, label=f"{tg}_xload")
        if st_t0 == 0:
            pw = ["p_oobc"]
            if t > 0:
                pw.append(f"m{t//4-1}_idxld")
            A(1, "gpsimd", lambda e: e.dma_start(
                idx_scr.rearrange("(p c) o -> p (c o)", c=48), oobc[:, :]),
              wait=pw, inc="gp", inc_n=16, label=f"st{t//4}_prefill")

        for b in range(2):
            subs = (2 * b, 2 * b + 1)
            for j, s in enumerate(subs):
                wt = [f"{tg}_xload"]
                if t == 0 and b == 0 and j == 0:
                    wt += ["p_ident", "p_sel", "p_dtwst"]
                if t > 0 and b == 0 and j == 0:
                    wt.append(f"t{t-1}_woutmm")
                    wt.append(f"t{t-1}_sp1")
                if b == 1 and j == 0:
                    wt.append(f"{tg}_trb0")
                    wt.append(f"{tg}_sp0")
                xs = x_sb[:, s * 2048:(s + 1) * 2048]
                A(1, "scalar", (lambda xs, dst, ac: lambda e: e.activation(
                    dst, xs, AF.Identity, accum_out=ac))(
                        xs, xnA[:, j * 2048:(j + 1) * 2048], stc(st1, 0, j)),
                  wait=wt, inc="sc")
                A(1, "scalar", (lambda xs, dst, ac: lambda e: e.activation(
                    dst, xs, AF.Square, accum_out=ac))(
                        xs, xn2nA[:, j * 2048:(j + 1) * 2048], stc(st1, 2, j)),
                  wait=[("sc", S.cnt["sc"])], inc="sc", label=f"{tg}_b{b}sums")
            A(1, "gpsimd", (lambda t, b: lambda e: e.dma_start(
                xn_tok.rearrange("(t j p) d -> t p j d", p=128, j=NSUB)[t][:, 2 * b:2 * b + 2, :],
                xnA.rearrange("p (j d) -> p j d", j=2)))(t, b),
              wait=[f"{tg}_b{b}sums2"], inc="ldx", inc_n=16, label=f"{tg}_sp{b}")
            A(1, "vector", lambda e: e.tensor_scalar_mul(st1[:, 4:6], st1[:, 0:2], 1.0 / D),
              wait=[f"{tg}_b{b}sums"], inc="ve")
            A(1, "vector", lambda e: e.tensor_scalar_mul(st1[:, 6:8], st1[:, 2:4], 1.0 / D),
              wait=[vsw()], inc="ve")
            A(1, "vector", lambda e: e.tensor_mul(st1[:, 8:10], st1[:, 4:6], st1[:, 4:6]),
              wait=[vsw()], inc="ve")
            A(1, "vector", lambda e: e.tensor_sub(st1[:, 8:10], st1[:, 6:8], st1[:, 8:10]),
              wait=[vsw()], inc="ve")
            A(1, "vector", lambda e: e.tensor_scalar_add(st1[:, 8:10], st1[:, 8:10], EPS),
              wait=[vsw()], inc="ve", label=f"{tg}_b{b}var")
            A(1, "scalar", lambda e: e.activation(st1[:, 14:16], st1[:, 8:10], AF.Ln),
              wait=[f"{tg}_b{b}var"], inc="sc")
            A(1, "scalar", lambda e: e.activation(st1[:, 10:12], st1[:, 14:16], AF.Exp, scale=-0.5),
              wait=[("sc", S.cnt["sc"])], inc="sc", label=f"{tg}_b{b}rstd")
            A(1, "vector", lambda e: e.scalar_tensor_tensor(
                st1[:, 12:14], st1[:, 4:6], -1.0, st1[:, 10:12], ALU.mult, ALU.mult),
              wait=[f"{tg}_b{b}rstd"], inc="ve", label=f"{tg}_b{b}nmr")
            for j, s in enumerate(subs):
                xs = x_sb[:, s * 2048:(s + 1) * 2048]
                xn_j = xnA[:, j * 2048:(j + 1) * 2048]
                A(1, "scalar", (lambda xs, dst, sc_, bi, ac: lambda e: e.activation(
                    dst, xs, AF.Identity, scale=sc_, bias=bi, accum_out=ac))(
                        xs, xn_j, stc(st1, 10, j), stc(st1, 12, j), stc(st2, 0, j)),
                  wait=[f"{tg}_b{b}nmr"], inc="sc")
                A(1, "scalar", (lambda src, dst, ac: lambda e: e.activation(
                    dst, src, AF.Square, accum_out=ac))(
                        xn_j, xn2nA[:, j * 2048:(j + 1) * 2048], stc(st2, 2, j)),
                  wait=[("sc", S.cnt["sc"])], inc="sc", label=f"{tg}_b{b}sums2")
            A(1, "vector", lambda e: e.tensor_scalar_mul(st2[:, 4:6], st2[:, 0:2], 1.0 / D),
              wait=[f"{tg}_b{b}sums2"], inc="ve")
            A(1, "vector", lambda e: e.tensor_scalar_mul(st2[:, 6:8], st2[:, 2:4], 1.0 / D),
              wait=[vsw()], inc="ve")
            A(1, "vector", lambda e: e.tensor_mul(st2[:, 8:10], st2[:, 4:6], st2[:, 4:6]),
              wait=[vsw()], inc="ve")
            A(1, "vector", lambda e: e.tensor_sub(st2[:, 8:10], st2[:, 6:8], st2[:, 8:10]),
              wait=[vsw()], inc="ve")
            A(1, "vector", lambda e: e.tensor_scalar_add(st2[:, 8:10], st2[:, 8:10], EPS),
              wait=[vsw()], inc="ve", label=f"{tg}_b{b}var2")
            A(1, "scalar", lambda e: e.activation(st2[:, 14:16], st2[:, 8:10], AF.Ln),
              wait=[f"{tg}_b{b}var2"], inc="sc")
            A(1, "scalar", lambda e: e.activation(st2[:, 10:12], st2[:, 14:16], AF.Exp, scale=-0.5),
              wait=[("sc", S.cnt["sc"])], inc="sc", label=f"{tg}_b{b}rstd2")
            A(1, "vector", lambda e: e.scalar_tensor_tensor(
                st2[:, 12:14], st2[:, 4:6], -1.0, st2[:, 10:12], ALU.mult, ALU.mult),
              wait=[f"{tg}_b{b}rstd2"], inc="ve", label=f"{tg}_b{b}nmr2")
            for j, s in enumerate(subs):
                xn_j = xnA[:, j * 2048:(j + 1) * 2048]
                A(1, "scalar", (lambda src, dst, sc_, bi: lambda e: e.activation(
                    dst, src, AF.Identity, scale=sc_, bias=bi))(
                        xn_j, xn2nA[:, j * 2048:(j + 1) * 2048],
                        stc(st2, 10, j), stc(st2, 12, j)),
                  wait=[f"{tg}_b{b}nmr2"], inc="sc",
                  label=f"{tg}_b{b}ln" if j == 1 else None)

            jj = 0
            for j, s in enumerate(subs):
                for c in range(16):
                    bnk = psum[jj % 2]
                    bnk2 = psum[2 + jj % 2]
                    wt = [f"{tg}_b{b}ln"]
                    if t > 0 and b == 0 and jj < 2:
                        wt.append(f"m{t//4-1}_comb_done" if t % 4 == 0
                                  else f"t{t-1}_ga15")
                    if b == 1 and jj < 2:
                        wt.append(f"{tg}_b0ev{30+jj}")
                    if jj >= 2:
                        wt.append(f"{tg}_b{b}ev{jj-2}")
                    A(1, "tensor", (lambda bnk, j, c: lambda e: e.transpose(
                        bnk[:, 0:128], xnA[:, j * 2048 + c * 128: j * 2048 + (c + 1) * 128],
                        ident[:, :]))(bnk, j, c),
                      wait=wt, inc="pe", label=f"{tg}_b{b}trx{jj}", psw={jj % 2})
                    wt2 = []
                    if b == 1 and jj < 2:
                        wt2.append(f"{tg}_b0ev2_{30+jj}")
                    if jj >= 2:
                        wt2.append(f"{tg}_b{b}ev2_{jj-2}")
                    A(1, "tensor", (lambda bnk2, j, c: lambda e: e.transpose(
                        bnk2[:, 0:128], xn2nA[:, j * 2048 + c * 128: j * 2048 + (c + 1) * 128],
                        ident[:, :]))(bnk2, j, c),
                      wait=wt2, inc="pe", label=f"{tg}_b{b}trx2_{jj}", psw={2 + jj % 2})
                    dst = slice(c * 512 + s * 128, c * 512 + (s + 1) * 128)
                    A(1, "vector", (lambda bnk, dst: lambda e: e.tensor_copy(
                        xnT_hi[:, dst], bnk[:, 0:128]))(bnk, dst),
                      wait=[("+1", f"{tg}_b{b}trx{jj}")], inc="ve", psr={jj % 2})
                    A(1, "vector", (lambda bnk, dst: lambda e: e.scalar_tensor_tensor(
                        xnT_lo[:, dst], xnT_hi[:, dst], -1.0, bnk[:, 0:128],
                        ALU.mult, ALU.add))(bnk, dst),
                      wait=[vsw()], inc="ve", label=f"{tg}_b{b}ev{jj}", psr={jj % 2})
                    wt3 = [f"{tg}_b{b}trx2_{jj}"]
                    if t == 0 and b == 0 and jj == 0:
                        wt3 += ["p_gcol", "p_bcol"]
                    wt3 = [("+1", f"{tg}_b{b}trx2_{jj}")] + wt3[1:]
                    A(1, "scalar", (lambda bnk2, dst, c: lambda e: e.activation(
                        xn2T[:, dst], bnk2[:, 0:128], AF.Identity,
                        scale=gcol[:, c:c + 1], bias=bcol[:, c:c + 1]))(bnk2, dst, c),
                      wait=wt3, inc="sc", label=f"{tg}_b{b}ev2_{jj}", psr={2 + jj % 2})
                    jj += 1
            A(1, "tensor", lambda e: e.drain(), wait=[], inc="pe")
            if b == 0:
                S.lab[f"{tg}_trb0"] = S.lab[f"{tg}_b0trx2_{jj-1}"]
        S.lab[f"{tg}_evall"] = ("ve", S.cnt["ve"])
        S.lab[f"{tg}_ev2all"] = ("sc", S.cnt["sc"])

        nmm = 0
        for c in range(16):
            for (lh, rh) in ((rT_hi, xnT_hi), (rT_hi, xnT_lo), (rT_lo, xnT_hi)):
                wt = []
                if nmm == 0:
                    wt = [f"{tg}_evall"]
                    if t > 0:
                        wt.append(f"m{t//4-1}_comb_done" if t % 4 == 0
                                  else f"t{t-1}_gp15")
                A(1, "tensor", (lambda lh, rh, c, nmm: lambda e: e.matmul(
                    psum[4][0:8, 0:512], lh[:, 8 * c:8 * c + 8],
                    rh[:, c * 512:(c + 1) * 512],
                    start=(nmm == 0), stop=(nmm == 47)))(lh, rh, c, nmm),
                  wait=wt, inc=None, psw={4})
                nmm += 1
        A(1, "tensor", lambda e: e.drain(), wait=[], inc="pe", label=f"{tg}_rmm")
        A(1, "vector", lambda e: e.tensor_copy(logitsT[:, :], psum[4][0:8, 0:512]),
          wait=[f"{tg}_rmm"], inc="ve", label=f"{tg}_lgev", psr={4})
        for s in range(NSUB):
            wt = [f"{tg}_lgev"]
            if t > 0 and s == 0:
                wt.append(f"t{t-1}_dmm1")
            A(1, "tensor", (lambda s: lambda e: e.transpose(
                psum[5][:, 8 * s:8 * s + 8], logitsT[0:8, 128 * s:128 * (s + 1)],
                ident[0:8, 0:8]))(s),
              wait=wt, inc="pe", label=None if s < 3 else None, psw={5})
            if s == 3:
                A(1, "tensor", lambda e: e.drain(), wait=[], inc="pe", label=f"{tg}_lgT3")
        A(1, "vector", lambda e: e.tensor_copy(L_tok[:, :], psum[5][:, 0:32]),
          wait=[f"{tg}_lgT3"], inc="ve", label=f"{tg}_ltok", psr={5})
        for s in range(NSUB):
            Ls = L_tok[:, 8 * s:8 * (s + 1)]
            A(1, "vector", (lambda Ls, s: lambda e: e.reduce_max(
                st1[:, s:s + 1], Ls, axis=AX.X))(Ls, s), wait=[vsw()], inc="ve")
            A(1, "vector", (lambda Ls, s: lambda e: e.tensor_scalar(
                oh1[:, 8 * s:8 * (s + 1)], Ls, st1[:, s:s + 1], None, ALU.is_ge))(Ls, s),
              wait=[vsw()], inc="ve")
            A(1, "vector", (lambda Ls, s: lambda e: e.scalar_tensor_tensor(
                lmsk[:, :], oh1[:, 8 * s:8 * (s + 1)], -1e30, Ls, ALU.mult, ALU.add))(Ls, s),
              wait=[vsw()], inc="ve")
            A(1, "vector", (lambda s: lambda e: e.reduce_max(
                st1[:, 4 + s:5 + s], lmsk[:, :], axis=AX.X))(s), wait=[vsw()], inc="ve")
            A(1, "vector", (lambda s: lambda e: e.tensor_scalar(
                oh2[:, 8 * s:8 * (s + 1)], lmsk[:, :], st1[:, 4 + s:5 + s], None, ALU.is_ge))(s),
              wait=[vsw()], inc="ve")
            A(1, "vector", (lambda s: lambda e: e.tensor_sub(
                st1[:, 8 + s:9 + s], st1[:, s:s + 1], st1[:, 4 + s:5 + s]))(s),
              wait=[vsw()], inc="ve", label=f"{tg}_d12_{s}")
        A(1, "scalar", lambda e: e.activation(st1[:, 12:16], st1[:, 8:12], AF.Sigmoid),
          wait=[f"{tg}_d12_3"], inc="sc")
        A(1, "scalar", lambda e: e.activation(st2[:, 12:16], st1[:, 8:12], AF.Sigmoid, scale=-1.0),
          wait=[("sc", S.cnt["sc"])], inc="sc", label=f"{tg}_sigs")
        for s in range(NSUB):
            A(1, "vector", (lambda s: lambda e: e.tensor_scalar_mul(
                w_tok[:, 8 * s:8 * (s + 1)], oh1[:, 8 * s:8 * (s + 1)],
                st1[:, 12 + s:13 + s]))(s),
              wait=[f"{tg}_sigs", vsw()], inc="ve")
            A(1, "vector", (lambda s: lambda e: e.scalar_tensor_tensor(
                w_tok[:, 8 * s:8 * (s + 1)], oh2[:, 8 * s:8 * (s + 1)],
                st2[:, 12 + s:13 + s], w_tok[:, 8 * s:8 * (s + 1)],
                ALU.mult, ALU.add))(s),
              wait=[vsw()], inc="ve", label=f"{tg}_wtok{s}")
        for s in range(NSUB):
            A(1, "tensor", (lambda s: lambda e: e.transpose(
                psum[6][0:8, 128 * s:128 * (s + 1)], w_tok[:, 8 * s:8 * (s + 1)],
                ident[:, :]))(s),
              wait=[f"{tg}_wtok{s}"], inc="pe", psw={6})
            if s == 3:
                A(1, "tensor", lambda e: e.drain(), wait=[], inc="pe", label=f"{tg}_wT3")
        A(1, "vector", lambda e: e.tensor_copy(wTf[:, :], psum[6][0:8, 0:512]),
          wait=[f"{tg}_wT3"], inc="ve", label=f"{tg}_wTev", psr={6})

        st_t = t % 4
        A(1, "vector", lambda e: e.tensor_single_scalar(maskT[:, :], wTf[:, :], 0.0, ALU.is_gt),
          wait=[vsw()] + ([("gp", S.cnt["gp"])] if st_t == 0 else []), inc="ve")
        ciw = [vsw()]
        if st_t == 0:
            A(1, "vector", lambda e: e.memset(cnt_init[:, 0:1], 0.0),
              wait=[vsw()] + ([f"t{t-1}_cumdone"] if t > 0 else []), inc="ve")
            ciw = [vsw()]
        A(1, "vector", lambda e: e.tensor_tensor_scan(
            cntf[:, :], maskT[:, :], ones8[:, :], cnt_init[:, 0:1], ALU.add, ALU.mult),
          wait=ciw, inc="ve", label=f"{tg}_scan")
        A(1, "vector", lambda e: e.tensor_copy(cnt_init[:, 0:1], cntf[:, 511:512]),
          wait=[vsw()], inc="ve", label=f"{tg}_cumdone")
        for s in range(NSUB):
            A(1, "tensor", (lambda s: lambda e: e.transpose(
                psum[7][:, 8 * s:8 * s + 8], cntf[0:8, 128 * s:128 * (s + 1)],
                ident[0:8, 0:8]))(s),
              wait=([f"{tg}_scan"] if s == 0 else []) +
                   ([f"t{t-1}_cntTev"] if (s == 0 and t > 0) else []) +
                   ([f"m{t//4-1}_comb_done"] if (s == 0 and t > 0 and t % 4 == 0) else []),
              inc=None, psw={7})
        A(1, "tensor", lambda e: e.drain(), wait=[], inc="pe", label=f"{tg}_cntT")
        A(1, "vector", lambda e: e.tensor_copy(cntT_sb[:, :], psum[7][:, 0:32]),
          wait=[f"{tg}_cntT"], inc="ve", label=f"{tg}_cntTev", psr={7})
        for s in range(NSUB):
            cs = cntT_sb[:, 8 * s:8 * (s + 1)]
            A(1, "vector", (lambda cs: lambda e: e.tensor_scalar_add(rt1[:, :], cs, -1.0))(cs),
              wait=[vsw()], inc="ve")
            A(1, "vector", lambda e: e.tensor_copy(rti[:, :], rt1[:, :]),
              wait=[vsw()], inc="ve")
            A(1, "vector", lambda e: e.tensor_single_scalar(rtb[:, :], rti[:, :], 7, ALU.arith_shift_right),
              wait=[vsw()], inc="ve")
            A(1, "vector", lambda e: e.tensor_single_scalar(rtr[:, :], rti[:, :], 127, ALU.bitwise_and),
              wait=[vsw()], inc="ve")
            A(1, "vector", lambda e: e.tensor_copy(rt2[:, :], rtr[:, :]),
              wait=[vsw()], inc="ve")
            A(1, "vector", lambda e: e.tensor_copy(rt3[:, :], rtb[:, :]),
              wait=[vsw()], inc="ve")
            A(1, "vector", lambda e: e.scalar_tensor_tensor(
                rt3[:, :], rt2[:, :], 48.0, rt3[:, :], ALU.mult, ALU.add),
              wait=[vsw()], inc="ve")
            A(1, "vector", lambda e: e.tensor_add(rt3[:, :], rt3[:, :], baseb2[:, :]),
              wait=[vsw()], inc="ve")
            A(1, "vector", lambda e: e.tensor_single_scalar(rt2[:, :], rt1[:, :], float(CAP) - 0.5, ALU.is_gt),
              wait=[vsw()], inc="ve")
            A(1, "vector", lambda e: e.scalar_tensor_tensor(
                rt3[:, :], rt2[:, :], 1e7, rt3[:, :], ALU.mult, ALU.add),
              wait=[vsw()], inc="ve")
            A(1, "vector", lambda e: e.scalar_tensor_tensor(
                rt1[:, :], baseb2[:, :], float(CAP) / 6.0, rt1[:, :], ALU.mult, ALU.add),
              wait=[vsw()], inc="ve")
            A(1, "vector", lambda e: e.scalar_tensor_tensor(
                rt1[:, :], rt2[:, :], 1e7, rt1[:, :], ALU.mult, ALU.add),
              wait=[vsw()], inc="ve", label=f"{tg}_enc{s}")
            for rk, ohx, svcol in ((0, oh1, st1), (1, oh2, st2)):
                kcol = st_t * 8 + s * 2 + rk
                A(1, "vector", (lambda ohx, s: lambda e: e.tensor_mul(
                    rt2[:, :], ohx[:, 8 * s:8 * (s + 1)], rt3[:, :]))(ohx, s),
                  wait=[vsw()], inc="ve")
                A(1, "vector", lambda e: e.reduce_sum(rt2[:, 0:1], rt2[:, :], axis=AX.X),
                  wait=[vsw()], inc="ve")
                A(1, "vector", (lambda kcol: lambda e: e.tensor_copy(
                    pos_keep[:, kcol:kcol + 1], rt2[:, 0:1]))(kcol),
                  wait=[vsw()], inc="ve", label=f"{tg}_penc{s}_{rk}")
                A(1, "vector", (lambda ohx, s: lambda e: e.tensor_mul(
                    rt2[:, :], ohx[:, 8 * s:8 * (s + 1)], rt1[:, :]))(ohx, s),
                  wait=[vsw()], inc="ve")
                A(1, "vector", (lambda kcol: lambda e: e.reduce_sum(
                    posf_keep[:, kcol:kcol + 1], rt2[:, :], axis=AX.X))(kcol),
                  wait=[vsw()], inc="ve")
                A(1, "vector", (lambda kcol: lambda e: e.tensor_copy(
                    pos_g[:, kcol:kcol + 1], posf_keep[:, kcol:kcol + 1]))(kcol),
                  wait=[vsw()], inc="ve")
                A(1, "vector", (lambda kcol: lambda e: e.tensor_single_scalar(
                    rt2[:, 0:1], posf_keep[:, kcol:kcol + 1], float(NE * CAP) - 0.5, ALU.is_lt))(kcol),
                  wait=[vsw()], inc="ve")
                A(1, "vector", (lambda kcol, svcol, s: lambda e: e.tensor_mul(
                    wv_keep[:, kcol:kcol + 1], svcol[:, 12 + s:13 + s], rt2[:, 0:1]))(kcol, svcol, s),
                  wait=[vsw()], inc="ve", label=f"{tg}_posw{s}_{rk}")
                A(1, "gpsimd", (lambda kcol, s, t: lambda e: e.indirect_dma_start(
                    idx_scr, IndirectOffsetOnAxis(pos_keep[:, kcol:kcol + 1], 0),
                    tokid_sb[:, t * 4 + s:t * 4 + s + 1], None,
                    bounds_check=dregs["yg"], oob_is_err=False))(kcol, s, t),
                  wait=[f"{tg}_penc{s}_{rk}", f"st{t // 4}_prefill"], inc="gp", inc_n=16,
                  label=f"{tg}_scat{s}_{rk}")

        for gp_ in range(ZCH):
            slot, si = slab_load(
                (lambda gp_: lambda slot: (lambda e: e.dma_start(
                    slot[:, :], winT[gp_])))(gp_),
                f"{tg}_dld{gp_}")
            pz, pg = [(psum[4], psum[5]), (psum[6], psum[7]), (psum[2], psum[3])][gp_ % 3]
            for k in range(16):
                wt = [f"{tg}_dld{gp_}"]
                if k == 0:
                    if gp_ < 3:
                        wt.append(f"{tg}_wTev")
                        wt.append(f"{tg}_ev2all")
                    else:
                        wt.append(f"{tg}_dact{gp_-3}")
                A(1, "tensor", (lambda slot, pz, k: lambda e: e.matmul(
                    pz[:, :], slot[:, k * 128:(k + 1) * 128],
                    xn2T[:, k * 512:(k + 1) * 512],
                    start=(k == 0), stop=(k == 15)))(slot, pz, k),
                  wait=wt, inc=None)
            for k in range(16):
                A(1, "tensor", (lambda slot, pg, k: lambda e: e.matmul(
                    pg[:, :], slot[:, 2048 + k * 128: 2048 + (k + 1) * 128],
                    xn2T[:, k * 512:(k + 1) * 512],
                    start=(k == 0), stop=(k == 15)))(slot, pg, k),
                  wait=[], inc="sfree" if k == 15 else None,
                  label=f"{tg}_dmm{gp_}" if k == 15 else None)
            if debug and t == 0 and gp_ == 0:
                dbg_zg = nc.dram_tensor("dbg_zg", [128, 1024], F32, kind="ExternalOutput").ap()
                zgs = nc.alloc_sbuf_tensor("zgs", [128, 1024], F32)
                A(1, "vector", (lambda pz: lambda e: e.tensor_copy(zgs[:, 0:512], pz[:, :]))(pz),
                  wait=[("sfree", si + 2)], inc="ve")
                A(1, "vector", (lambda pg: lambda e: e.tensor_copy(zgs[:, 512:1024], pg[:, :]))(pg),
                  wait=[("ve", S.cnt["ve"])], inc="ve", label="dbg_zgcp")
                A(1, "gpsimd", (lambda dbg_zg: lambda e: e.dma_start(dbg_zg, zgs[:, :]))(dbg_zg),
                  wait=["dbg_zgcp"], inc="stx", inc_n=16)
            if gp_ == ZCH - 1:
                A(1, "tensor", lambda e: e.drain(), wait=[], inc="pe", label=f"{tg}_dend")
            A(1, "vector", (lambda pz: lambda e: e.tensor_scalar(
                T0, pz[:, :], 15.0, -15.0, ALU.min, ALU.max))(pz),
              wait=([("sfree", si + 2)] if gp_ < ZCH - 1 else [f"{tg}_dend"]),
              inc="ve", label=f"{tg}_dcz{gp_}")
            A(1, "scalar", lambda e: e.activation(T1, T0, AF.Sigmoid),
              wait=[f"{tg}_dcz{gp_}"], inc="sc", label=f"{tg}_dsz{gp_}")
            A(1, "vector", (lambda pz, gp_: lambda e: e.scalar_tensor_tensor(
                T2, pz[:, :], dtw_col[:, gp_:gp_ + 1], T1, ALU.mult, ALU.mult))(pz, gp_),
              wait=[f"{tg}_dsz{gp_}"], inc="ve", label=f"{tg}_dzt{gp_}")
            A(1, "vector", (lambda pg: lambda e: e.tensor_scalar(
                T3, pg[:, :], 15.0, -15.0, ALU.min, ALU.max))(pg),
              wait=[vsw()], inc="ve", label=f"{tg}_dcg{gp_}")
            A(1, "scalar", lambda e: e.activation(T1, T3, AF.Sigmoid),
              wait=[f"{tg}_dcg{gp_}", f"{tg}_dzt{gp_}"], inc="sc", label=f"{tg}_dsg{gp_}")
            A(1, "vector", lambda e: e.tensor_mul(T0, T3, T1),
              wait=[f"{tg}_dsg{gp_}"], inc="ve")
            A(1, "vector", (lambda gp_: lambda e: e.tensor_mul(
                act_s[:, gp_ * 512:(gp_ + 1) * 512], T2, T0))(gp_),
              wait=[vsw()], inc="ve", label=f"{tg}_dact{gp_}")

        if debug and t == 0:
            dbg_xh0 = nc.dram_tensor("dbg_xh0", [128, 512], BF16, kind="ExternalOutput").ap()
            dbg_xl0 = nc.dram_tensor("dbg_xl0", [128, 512], BF16, kind="ExternalOutput").ap()
            A(1, "gpsimd", (lambda dbg_xh0: lambda e: e.dma_start(dbg_xh0, xnT_hi[:, 0:512]))(dbg_xh0),
              wait=[f"{tg}_dmm{ZCH-1}"], inc="stx", inc_n=16)
            A(1, "gpsimd", (lambda dbg_xl0: lambda e: e.dma_start(dbg_xl0, xnT_lo[:, 0:512]))(dbg_xl0),
              wait=[f"{tg}_dmm{ZCH-1}"], inc="stx", inc_n=16, label="dbg_xtap")
        if debug and t == 0:
            dbg_x2 = nc.dram_tensor("dbg_xn2T", [128, 8192], BF16, kind="ExternalOutput").ap()
            A(1, "gpsimd", (lambda dbg_x2: lambda e: e.dma_start(dbg_x2, PC[:, :]))(dbg_x2),
              wait=[f"{tg}_dmm{ZCH-1}"], inc="stx", inc_n=16, label="dbg_x2tap")
            dbg_actD = nc.dram_tensor("dbg_actD", [128, 16384], BF16, kind="ExternalOutput").ap()
            A(1, "gpsimd", (lambda dbg_actD: lambda e: e.dma_start(dbg_actD, act_s[:, :]))(dbg_actD),
              wait=[f"{tg}_dact{ZCH-1}"], inc="stx", inc_n=16, label="dbg_acttap")

        g1_transposes = g1_transposes_factory(tg, t)
        for dc in range(16):
            slot, si = slab_load(
                (lambda dc: lambda slot: (lambda e: e.dma_start(
                    slot[:, :], woutT[dc])))(dc),
                f"{tg}_gld{dc}")
            sb_ = 4 + dc % 3
            pS = psum[sb_]
            for k in range(32):
                wt = [f"{tg}_gld{dc}"]
                if k == 0:
                    if dc < 3:
                        wt.append(f"{tg}_dact{ {0: 30, 1: 31, 2: 29}[dc] }")
                    else:
                        wt.append(f"{tg}_gp{dc-3}")
                A(1, "tensor", (lambda slot, pS, k: lambda e: e.matmul(
                    pS[:, :], slot[:, k * 128:(k + 1) * 128],
                    act_s[:, k * 512:(k + 1) * 512],
                    start=(k == 0), stop=(k == 31)))(slot, pS, k),
                  wait=wt if k == 0 else
                       ([f"{tg}_dact31"] if (dc == 0 and k == 31) else []),
                  inc="sfree" if k == 31 else None,
                  label=f"{tg}_gmm{dc}" if k == 31 else None, psw={sb_})
            if dc == 15:
                S.lab[f"{tg}_woutmm"] = ("sfree", S.cnt["sfree"])
            if dc >= 1:
                g1_transposes(dc - 1)
            A(1, "vector", (lambda pS, dc: lambda e: e.scalar_tensor_tensor(
                T0, pS[:, :], 1.0, xnT_hi[:, dc * 512:(dc + 1) * 512], ALU.mult, ALU.add))(pS, dc),
              wait=([("sfree", si + 2)] if dc < 14 else [f"{tg}_gq{dc-1}"]) +
                   ([f"{tg}_ga{dc-2}"] if dc >= 2 else []),
              inc="ve", psr={sb_})
            A(1, "vector", (lambda dc: lambda e: e.tensor_add(
                T2g[dc % 2][:, :], T0, xnT_lo[:, dc * 512:(dc + 1) * 512]))(dc),
              wait=[vsw()], inc="ve", label=f"{tg}_gp{dc}")
        g1_transposes(15)
        A(1, "tensor", lambda e: e.drain(), wait=[], inc="pe", label=f"{tg}_g1end")
        S.lab[f"{tg}_sdone"] = ("ve", S.cnt["ve"])
        A(1, "gpsimd", (lambda t: lambda e: e.dma_start(
            s_tok.rearrange("(t s p) d -> t p s d", p=128, s=NSUB)[t],
            s_stage.rearrange("p (s d) -> p s d", s=NSUB)))(t),
          wait=[f"{tg}_sdone"], inc="ldx", inc_n=16, label=f"{tg}_sspill")

        if st_t == 3:
            emit_moe_supertile(t // 4)

    if debug:
        dbg_specs = [
            ("dbg_xnT_hi", xnT_hi[:, :], BF16, [128, 8192]),
            ("dbg_xnT_lo", xnT_lo[:, :], BF16, [128, 8192]),
            ("dbg_PC", PC[:, :], BF16, [128, 8192]),
            ("dbg_act", act_s[:, :], BF16, [128, 16384]),
            ("dbg_Ltok", L_tok[:, :], F32, [128, 32]),
            ("dbg_posg", pos_g[:, :], I32, [128, 32]),
            ("dbg_posf", posf_keep[:, :], F32, [128, 32]),
            ("dbg_penc", pos_keep[:, :], I32, [128, 32]),
            ("dbg_wvk", wv_keep[:, :], F32, [128, 32]),
            ("dbg_cntT", cntT_sb[:, :], F32, [128, 32]),
            ("dbg_wtok", w_tok[:, :], F32, [128, 32]),
            ("dbg_wT", wT[:, :], BF16, [8, 512]),
            ("dbg_wsb", w_sb[:, :], F32, [128, 512]),
            ("dbg_dtw", dtw_col[:, :], F32, [128, 32]),
            ("dbg_gcol", gcol[:, :], F32, [128, 16]),
            ("dbg_bcol", bcol[:, :], F32, [128, 16]),
            ("dbg_st1", st1[:, :], F32, [128, 16]),
            ("dbg_st2", st2[:, :], F32, [128, 16]),
            ("dbg_logT", logitsT[:, :], F32, [8, 512]),
            ("dbg_out", out_sb, F32, [128, 8192]),
            ("dbg_xsb", x_sb, F32, [128, 8192]),
            ("dbg_rThi", rT_hi[:, :], BF16, [128, 128]),
            ("dbg_ident", ident[:, :], F32, [128, 128]),
            ("dbg_selt", selt[:, :], BF16, [8, 1024]),
        ]
        for nm, src_ap, shp in [("dbg_winT", winT.rearrange("a p c -> (a p) c"), [32 * 128, 4096]),
                                 ("dbg_xntok", xn_tok, [TCORE, D]),
                                 ("dbg_stok", s_tok, [TCORE, D]),
                                 ("dbg_ygscr", yg_scr, [NE * 768, D]),
                                 ("dbg_idxscr", idx_scr.rearrange("a b -> (a b)").rearrange("(a b) -> a b", b=1), [NE * 768, 1])]:
            ddt = I32 if nm == "dbg_idxscr" else BF16
            dh = nc.dram_tensor(nm, shp, ddt, kind="ExternalOutput").ap()
            A(1, "gpsimd", (lambda dh, src_ap: lambda e: e.dma_start(dh, src_ap))(dh, src_ap),
              wait=[("stx", S.cnt["stx"])], inc="stx", inc_n=16)
        lastv = ("ve", S.cnt["ve"])
        lastc = ("sc", S.cnt["sc"])
        lastp = ("pe", S.cnt["pe"])
        for nm, ap, dt_, shp in dbg_specs:
            dh = nc.dram_tensor(nm, shp, dt_, kind="ExternalOutput").ap()
            A(1, "gpsimd", (lambda dh, ap: lambda e: e.dma_start(dh, ap))(dh, ap),
              wait=[lastv, lastc, lastp, ("stx", S.cnt["stx"])], inc="stx", inc_n=16)

    A(1, "gpsimd", lambda e: e.nop(), wait=[("stx", S.cnt["stx"]), ("ve", S.cnt["ve"]),
                                            ("sc", S.cnt["sc"]), ("pe", S.cnt["pe"])], inc=None)

    S.resolve_waits()
    S.verify()

    sems = {}
    from contextlib import ExitStack
    es = ExitStack()
    with es:
        for s in sem_names:
            sems[s] = es.enter_context(nc.semaphore(s))
        for blk in (0, 1):
            if all(not S.ops[blk][e] for e in ENGINES):
                continue
            with nc.Block() as block:
                for eng_name in ENGINES:
                    ops = S.ops[blk][eng_name]
                    if not ops:
                        continue

                    def body(eng, ops=ops):
                        for fn, waits, incsem, inc_n, _, inc2, _pw, _pr in ops:
                            for s, v in waits:
                                eng.wait_ge(sems[s], v)
                            inst = fn(eng)
                            if incsem is not None:
                                inst = inst.then_inc(sems[incsem], inc_n)
                            if inc2 is not None:
                                inst.then_inc(sems[inc2], 1)
                    getattr(block, eng_name)(body)
    return nc


_CACHE = {}


def _get_nc(n_tiles=NT):
    if n_tiles not in _CACHE:
        _CACHE[n_tiles] = build_program(n_tiles)
    return _CACHE[n_tiles]


def kernel(L0_attn_x, L0_attn_gamma, L0_attn_beta, L0_attn_W_in, L0_attn_W_out,
           L0_attn_dt, L0_ffn_router_gate, L0_ffn_expert_gate_up, L0_ffn_expert_down):
    nc = _get_nc()
    xf = np.ascontiguousarray(np.asarray(L0_attn_x, dtype=np.float32).reshape(NCORES * TCORE, D))
    shared = {
        "L0_attn_gamma": np.ascontiguousarray(L0_attn_gamma, dtype=np.float32),
        "L0_attn_beta": np.ascontiguousarray(L0_attn_beta, dtype=np.float32),
        "L0_attn_W_in": np.ascontiguousarray(L0_attn_W_in, dtype=np.float32),
        "L0_attn_W_out": np.ascontiguousarray(L0_attn_W_out, dtype=np.float32),
        "L0_attn_dt": np.ascontiguousarray(L0_attn_dt, dtype=np.float32),
        "L0_ffn_router_gate": np.ascontiguousarray(L0_ffn_router_gate, dtype=np.float32),
        "L0_ffn_expert_gate_up": np.ascontiguousarray(L0_ffn_expert_gate_up, dtype=np.float32),
        "L0_ffn_expert_down": np.ascontiguousarray(L0_ffn_expert_down, dtype=np.float32),
    }
    in_maps = []
    for c in range(NCORES):
        m = dict(shared)
        m["L0_attn_x"] = np.ascontiguousarray(xf[c * TCORE:(c + 1) * TCORE])
        in_maps.append(m)
    res = run_bass_kernel_spmd(nc, in_maps, core_ids=list(range(NCORES)))
    out = np.concatenate([r["out"] for r in res.results], axis=0)
    return out.reshape(4, 8192, D)
